# version 03 (final)
# baseline (speedup 1.0000x reference)
import numpy as np
import ml_dtypes
import concourse.bass as bass
import concourse.mybir as mybir
from concourse.bass_utils import run_bass_kernel_spmd
from contextlib import ExitStack

F32 = mybir.dt.float32
F32R = mybir.dt.float32r
BF16 = mybir.dt.bfloat16
AF = mybir.ActivationFunctionType
ALU = mybir.AluOpType
AX = mybir.AxisListType.X

D = 2048
S = 4096
B = 4
NCORE = 8
TSH = 2048
TT = 512
NCH = D // 128
LN_EPS = 1e-5
RMS_EPS = 1e-6
BF = ml_dtypes.bfloat16


class Emitter:

    def __init__(self, live, sems):
        self.live = live
        self.sems = sems
        self.cnt = {k: 0 for k in sems}

    def do(self, eng, fname, *args, inc=None, winc=1, **kw):
        e = self.live.get(eng)
        inst = None
        if e is not None:
            inst = getattr(e, fname)(*args, **kw)
            if inc is not None:
                inst.then_inc(self.sems[inc], winc)
        if inc is not None:
            self.cnt[inc] += winc
        return inst

    def wait(self, eng, sem, val=None):
        v = self.cnt[sem] if val is None else val
        e = self.live.get(eng)
        if e is not None:
            e.wait_ge(self.sems[sem], v)

    def dma(self, eng, out, in_, inc):
        self.do(eng, "dma_start", out=out, in_=in_, inc=inc, winc=16)


def run_sections(nc, sems_names, program):
    ctx = ExitStack()
    with ctx:
        sems = {n: ctx.enter_context(nc.semaphore(n)) for n in sems_names}
        block = ctx.enter_context(nc.Block())

        def live(**kw):
            d = dict(sync=None, gp=None, pe=None, dve=None, act=None)
            d.update(kw)
            return d

        @block.sync
        def _(e):
            program(Emitter(live(sync=e), sems))

        @block.gpsimd
        def _(e):
            program(Emitter(live(gp=e), sems))

        @block.tensor
        def _(e):
            program(Emitter(live(pe=e), sems))

        @block.vector
        def _(e):
            program(Emitter(live(dve=e), sems))

        @block.scalar
        def _(e):
            program(Emitter(live(act=e), sems))


def chunked(t_ap, nch=NCH):
    return t_ap.rearrange("(c p) t -> p c t", p=128)


def emit_ln(em, xt, sq, yt, psA, one_c, one_r, small, tw=TT, one_sq=None, ysl=None):
    m_r, msq_r, t_r, u_r, rstd_r, rm_r = (small[k] for k in
                                          ("m", "msq", "t", "u", "rstd", "rm"))
    em.do("act", "activation", sq[:, :], xt[:, :], AF.Square, inc="act")
    em.wait("pe", "act")
    for c in range(NCH):
        em.do("pe", "matmul", psA[0:1, 0:512], one_c[:, :],
              xt[:, c * tw:(c + 1) * tw], start=(c == 0), stop=(c == NCH - 1))
    osq = one_c if one_sq is None else one_sq
    for c in range(NCH):
        em.do("pe", "matmul", psA[0:1, 512:1024], osq[:, :],
              sq[:, c * tw:(c + 1) * tw], start=(c == 0), stop=(c == NCH - 1),
              inc=("pe" if c == NCH - 1 else None))
    em.wait("dve", "pe")
    em.do("dve", "tensor_scalar", m_r[:, :], psA[0:1, 0:512], 1.0 / D, None, ALU.mult)
    em.do("dve", "tensor_tensor", msq_r[:, :], m_r[:, :], m_r[:, :], ALU.mult)
    em.do("dve", "scalar_tensor_tensor", t_r[:, :], psA[0:1, 512:1024],
          1.0 / D, msq_r[:, :], ALU.mult, ALU.subtract)
    em.do("dve", "tensor_scalar_add", t_r[:, :], t_r[:, :], LN_EPS, inc="dve")
    em.wait("act", "dve")
    em.do("act", "activation", u_r[:, :], t_r[:, :], AF.Ln)
    em.do("act", "activation", rstd_r[:, :], u_r[:, :], AF.Exp, scale=-0.5, inc="act")
    em.wait("dve", "act")
    em.do("dve", "tensor_tensor", rm_r[:, :], m_r[:, :], rstd_r[:, :], ALU.mult,
          inc="dve")
    em.wait("pe", "dve")
    em.do("pe", "matmul", psA[0:128, 1024:1536], one_r[:, :], rstd_r[:, :],
          start=True, stop=True)
    em.do("pe", "matmul", psA[0:128, 1536:2048], one_r[:, :], rm_r[:, :],
          start=True, stop=True, inc="pe")
    em.wait("dve", "pe")
    for c in range(NCH):
        yd = ysl(c) if ysl is not None else yt[:, c * tw:(c + 1) * tw]
        em.do("dve", "tensor_tensor", yd, xt[:, c * tw:(c + 1) * tw],
              psA[0:128, 1024:1536], ALU.mult)
        em.do("dve", "tensor_tensor", yd, yd,
              psA[0:128, 1536:2048], ALU.subtract,
              inc=("dve" if c == NCH - 1 else None))


def ln_smalls(nc, ctx, pfx, tw=TT):
    d = {}
    for k, dt in (("m", F32), ("msq", F32), ("t", F32), ("u", F32),
                  ("rstd", F32R), ("rm", F32R)):
        d[k] = ctx.enter_context(nc.sbuf_tensor(f"{pfx}_{k}", [1, tw], dt))
    return d



def build_a0a():
    HD = 64
    NH_L = 16
    NG_L = 4
    nc = bass.Bass("TRN2", target_bir_lowering=False)
    x_d = nc.dram_tensor("x", [D, S], F32R, kind="ExternalInput").ap()
    wq_d = nc.dram_tensor("wq", [D, NH_L * HD], F32, kind="ExternalInput").ap()
    wk_d = nc.dram_tensor("wk", [D, NG_L * HD], F32, kind="ExternalInput").ap()
    wv_d = nc.dram_tensor("wv", [D, NG_L * HD], F32, kind="ExternalInput").ap()
    onec_d = nc.dram_tensor("onec", [128, 1], F32R, kind="ExternalInput").ap()
    oner_d = nc.dram_tensor("oner", [1, 128], F32R, kind="ExternalInput").ap()
    selu_d = nc.dram_tensor("selu", [128, 1], BF16, kind="ExternalInput").ap()
    sell_d = nc.dram_tensor("sell", [128, 1], BF16, kind="ExternalInput").ap()
    idnb_d = nc.dram_tensor("idnb", [128, 128], BF16, kind="ExternalInput").ap()
    zmask_d = nc.dram_tensor("zmask", [128, 1024], BF16, kind="ExternalInput").ap()
    onecb_d = nc.dram_tensor("onecb", [128, 1], BF16, kind="ExternalInput").ap()
    ao_d = nc.dram_tensor("ao", [NH_L * HD, S], F32, kind="ExternalOutput").ap()
    q_dram = nc.dram_tensor("q_scr", [8, 128, S], BF16).ap()

    ctx = ExitStack()
    with ctx:
        wq_s = ctx.enter_context(nc.sbuf_tensor("wq_s", [128, NCH * 1024], BF16))
        wk_s = ctx.enter_context(nc.sbuf_tensor("wk_s", [128, NCH * 256], BF16))
        wv_s = ctx.enter_context(nc.sbuf_tensor("wv_s", [128, NCH * 256], BF16))
        one_c = ctx.enter_context(nc.sbuf_tensor("one_c", [128, 1], F32R))
        one_r = ctx.enter_context(nc.sbuf_tensor("one_r", [1, 128], F32R))
        selu = ctx.enter_context(nc.sbuf_tensor("selu_s", [128, 1], BF16))
        sell = ctx.enter_context(nc.sbuf_tensor("sell_s", [128, 1], BF16))
        idnb = ctx.enter_context(nc.sbuf_tensor("idnb_s", [128, 128], BF16))
        zmask = ctx.enter_context(nc.sbuf_tensor("zmask_s", [128, 1024], BF16))
        xt = ctx.enter_context(nc.sbuf_tensor("xt", [128, NCH * TT], F32R))
        sq = ctx.enter_context(nc.sbuf_tensor("sq", [128, NCH * TT], BF16))
        yt = ctx.enter_context(nc.sbuf_tensor("yt", [128, NCH * TT], BF16))
        onecb = ctx.enter_context(nc.sbuf_tensor("onecb_s", [128, 1], BF16))
        k_g = [ctx.enter_context(nc.sbuf_tensor(f"k_g{g}", [64, S], BF16))
               for g in range(4)]
        v_tm = ctx.enter_context(nc.sbuf_tensor("v_tm", [128, 32 * 260], BF16))
        qtmp = [ctx.enter_context(nc.sbuf_tensor(f"qtmp{i}", [128, 512], BF16))
                for i in range(2)]
        sqq = ctx.enter_context(nc.sbuf_tensor("sqq", [128, 512], BF16))
        vtmp = ctx.enter_context(nc.sbuf_tensor("vtmp", [128, 512], BF16))
        qm_row = ctx.enter_context(nc.sbuf_tensor("qm_row", [1, NH_L], F32))
        km_row = ctx.enter_context(nc.sbuf_tensor("km_row", [1, NG_L], F32))
        mtmp = ctx.enter_context(nc.sbuf_tensor("mtmp", [1, 1], F32))
        qs_row = ctx.enter_context(nc.sbuf_tensor("qs_row", [1, NH_L], F32))
        ks_row = ctx.enter_context(nc.sbuf_tensor("ks_row", [1, NG_L], F32))
        negc = ctx.enter_context(nc.sbuf_tensor("negc", [1, NH_L], F32R))
        nC = ctx.enter_context(nc.sbuf_tensor("nC", [128, NH_L], F32))
        qq = [ctx.enter_context(nc.sbuf_tensor(f"qq{i}", [64, 512], BF16))
              for i in range(2)]
        pt = [ctx.enter_context(nc.sbuf_tensor(f"pt{i}", [128, 512], BF16))
              for i in range(2)]
        l_r = ctx.enter_context(nc.sbuf_tensor("l_r", [1, 512], F32))
        rl_r = ctx.enter_context(nc.sbuf_tensor("rl_r", [1, 512], F32R))
        rlb = ctx.enter_context(nc.sbuf_tensor("rlb", [64, 512], F32))
        ao_t = [ctx.enter_context(nc.sbuf_tensor(f"ao_t{i}", [64, 512], F32))
                for i in range(2)]
        smalls = ln_smalls(nc, ctx, "ln")
        psA = ctx.enter_context(nc.psum_tensor("psA", [128, 2048], F32))
        psB = ctx.enter_context(nc.psum_tensor("psB", [128, 1024], F32))
        psN = ctx.enter_context(nc.psum_tensor("psN", [1, 512], F32))
        psTb = ctx.enter_context(nc.psum_tensor("psTb", [128, 512], BF16))

        NTT = S // TT

        def prog(em: Emitter):
            em.dma("gp", wq_s[:, :].rearrange("p (c o) -> p c o", c=NCH), chunked(wq_d), "ld")
            em.dma("gp", wk_s[:, :].rearrange("p (c o) -> p c o", c=NCH), chunked(wk_d), "ld")
            em.dma("gp", wv_s[:, :].rearrange("p (c o) -> p c o", c=NCH), chunked(wv_d), "ld")
            em.dma("sync", one_c[:, :], onec_d, "ld")
            em.dma("sync", one_r[:, :], oner_d, "ld")
            em.dma("sync", selu[:, :], selu_d, "ld")
            em.dma("sync", sell[:, :], sell_d, "ld")
            em.dma("sync", idnb[:, :], idnb_d, "ld")
            em.dma("sync", zmask[:, :], zmask_d, "ld")
            em.dma("sync", onecb[:, :], onecb_d, "ld")
            em.do("gp", "memset",
                  v_tm[:, :].rearrange("p (g c) -> p g c", c=65)[:, :, 64:65], 1.0,
                  inc="dve")
            em.do("gp", "memset", qm_row[:, :], 0.0)
            em.do("gp", "memset", km_row[:, :], 0.0, inc="dve")

            xv = chunked(x_d)
            for tt in range(NTT):
                t0 = tt * TT
                if tt > 0:
                    em.wait("sync", "pe")
                    em.wait("sync", "dve")
                em.dma("sync", xt[:, :].rearrange("p (c t) -> p c t", c=NCH),
                       xv[:, :, t0:t0 + TT], "ld")
                em.wait("act", "ld")
                emit_ln(em, xt, sq, yt, psA, one_c, one_r, smalls, one_sq=onecb)

                em.wait("pe", "ld")
                em.wait("pe", "dve")
                for oc in range(8):
                    sl = slice((oc % 2) * 512, (oc % 2) * 512 + 512)
                    if oc >= 2:
                        em.wait("pe", "dve")
                    for c in range(NCH):
                        em.do("pe", "matmul", psB[0:128, sl],
                              wq_s[:, c * 1024 + oc * 128:c * 1024 + (oc + 1) * 128],
                              yt[:, c * TT:(c + 1) * TT],
                              start=(c == 0), stop=(c == NCH - 1),
                              inc=("pe" if c == NCH - 1 else None))
                    em.wait("dve", "pe")
                    em.wait("dve", "st")
                    em.do("dve", "tensor_copy", qtmp[oc % 2][:, :], psB[0:128, sl],
                          inc="dve")
                    em.wait("act", "dve")
                    em.do("act", "activation", sqq[:, :], qtmp[oc % 2][:, :], AF.Square,
                          inc="act")
                    em.wait("pe", "act")
                    for half, sel in ((0, selu), (1, sell)):
                        hl = 2 * oc + half
                        em.do("pe", "matmul", psN[0:1, :], sel[:, :], sqq[:, :],
                              start=True, stop=True, inc="pe")
                        em.wait("dve", "pe")
                        em.do("dve", "reduce_max", mtmp[:, :], psN[0:1, :], axis=AX)
                        em.do("dve", "tensor_tensor", qm_row[0:1, hl:hl + 1],
                              qm_row[0:1, hl:hl + 1], mtmp[:, :], ALU.max, inc="dve")
                        if half == 0:
                            em.wait("pe", "dve")
                    em.wait("gp", "dve")
                    em.dma("gp", q_dram[oc, :, t0:t0 + TT], qtmp[oc % 2][:, :], "st")

                for oc in range(2):
                    sl = slice((oc % 2) * 512, (oc % 2) * 512 + 512)
                    em.wait("pe", "dve")
                    for c in range(NCH):
                        em.do("pe", "matmul", psB[0:128, sl],
                              wk_s[:, c * 256 + oc * 128:c * 256 + (oc + 1) * 128],
                              yt[:, c * TT:(c + 1) * TT],
                              start=(c == 0), stop=(c == NCH - 1),
                              inc=("pe" if c == NCH - 1 else None))
                    em.wait("dve", "pe")
                    em.do("dve", "tensor_copy", k_g[2 * oc][:, t0:t0 + TT],
                          psB[0:64, sl])
                    em.do("dve", "tensor_copy", k_g[2 * oc + 1][:, t0:t0 + TT],
                          psB[64:128, sl], inc="dve")
                    for half in range(2):
                        gl = 2 * oc + half
                        em.wait("act", "dve" if half == 0 else "pe")
                        em.do("act", "activation", sqq[0:64, 0:512],
                              k_g[gl][:, t0:t0 + TT], AF.Square, inc="act")
                        em.wait("pe", "act")
                        em.do("pe", "matmul", psN[0:1, :], selu[0:64, 0:1],
                              sqq[0:64, 0:512], start=True, stop=True, inc="pe")
                        em.wait("dve", "pe")
                        em.do("dve", "reduce_max", mtmp[:, :], psN[0:1, :], axis=AX)
                        em.do("dve", "tensor_tensor", km_row[0:1, gl:gl + 1],
                              km_row[0:1, gl:gl + 1], mtmp[:, :], ALU.max, inc="dve")

                for oc in range(2):
                    sl = slice((oc % 2) * 512, (oc % 2) * 512 + 512)
                    em.wait("pe", "dve")
                    for c in range(NCH):
                        em.do("pe", "matmul", psB[0:128, sl],
                              wv_s[:, c * 256 + oc * 128:c * 256 + (oc + 1) * 128],
                              yt[:, c * TT:(c + 1) * TT],
                              start=(c == 0), stop=(c == NCH - 1),
                              inc=("pe" if c == NCH - 1 else None))
                    em.wait("dve", "pe")
                    em.do("dve", "tensor_copy", vtmp[:, :], psB[0:128, sl], inc="dve")
                    em.wait("pe", "dve")
                    for j in range(4):
                        em.do("pe", "transpose", psTb[:, j * 128:(j + 1) * 128],
                              vtmp[:, j * 128:(j + 1) * 128], idnb[:, :],
                              inc=("pe" if j == 3 else None))
                    em.wait("dve", "pe")
                    for j in range(4):
                        tc = tt * 4 + j
                        em.do("dve", "tensor_copy",
                              v_tm[:, tc * 260 + (2 * oc) * 65:tc * 260 + (2 * oc) * 65 + 64],
                              psTb[:, j * 128:j * 128 + 64])
                        em.do("dve", "tensor_copy",
                              v_tm[:, tc * 260 + (2 * oc + 1) * 65:tc * 260 + (2 * oc + 1) * 65 + 64],
                              psTb[:, j * 128 + 64:(j + 1) * 128],
                              inc=("dve" if j == 3 else None))

            em.wait("act", "dve")
            em.do("act", "activation", qs_row[:, :], qm_row[:, :], AF.Sqrt)
            em.do("act", "activation", ks_row[:, :], km_row[:, :], AF.Sqrt, inc="act")
            em.wait("dve", "act")
            for g in range(NG_L):
                em.do("dve", "tensor_scalar", negc[0:1, 4 * g:4 * g + 4],
                      qs_row[0:1, 4 * g:4 * g + 4], ks_row[0:1, g:g + 1], -0.125,
                      ALU.mult, ALU.mult, inc=("dve" if g == NG_L - 1 else None))
            em.wait("pe", "dve")
            em.do("pe", "matmul", psB[0:128, 0:NH_L], one_r[:, :], negc[:, :],
                  start=True, stop=True, inc="pe")
            em.wait("dve", "pe")
            em.do("dve", "tensor_copy", nC[:, :], psB[0:128, 0:NH_L], inc="dve")

            for hl in range(NH_L):
                gl = hl // 4
                qch, qb = hl // 2, (hl % 2) * 64
                for qt in range(8):
                    nkc = 4 * (qt + 1)
                    em.wait("sync", "pe")
                    em.wait("sync", "st")
                    em.dma("sync", qq[qt % 2][:, :],
                           q_dram[qch, qb:qb + 64, qt * 512:(qt + 1) * 512], "ld")
                    em.wait("pe", "ld")
                    for kc in range(nkc):
                        koff = kc * 128
                        b = kc % 2
                        ssl = slice(b * 512, b * 512 + 512)
                        if kc >= 2:
                            em.wait("pe", "act")
                        em.do("pe", "matmul", psA[0:64, ssl],
                              k_g[gl][:, koff:koff + 64],
                              qq[qt % 2][:, :], start=True, stop=True)
                        em.do("pe", "matmul", psA[64:128, ssl],
                              k_g[gl][:, koff + 64:koff + 128],
                              qq[qt % 2][:, :], start=True, stop=True, inc="pe")
                        em.wait("act", "pe")
                        masked = (koff >= qt * 512)
                        em.do("act", "activation", pt[b][:, :], psA[0:128, ssl], AF.Exp,
                              bias=nC[:, hl:hl + 1], scale=0.125, inc="act")
                        if masked:
                            rel = koff - qt * 512
                            em.wait("dve", "act")
                            em.do("dve", "tensor_tensor", pt[b][:, :], pt[b][:, :],
                                  zmask[:, 512 - rel:1024 - rel], ALU.mult, inc="dve")
                            em.wait("pe", "dve")
                        else:
                            em.wait("pe", "act")
                        if kc == 0:
                            em.wait("pe", "dve")
                        em.do("pe", "matmul", psA[0:65, 1024:1536],
                              v_tm[:, kc * 260 + gl * 65:kc * 260 + (gl + 1) * 65],
                              pt[b][:, :], start=(kc == 0), stop=(kc == nkc - 1),
                              inc="pe")
                    em.wait("dve", "pe")
                    em.do("dve", "tensor_scalar_add", l_r[:, :], psA[64:65, 1024:1536],
                          1e-30)
                    em.do("dve", "reciprocal", l_r[:, :], l_r[:, :])
                    em.do("dve", "tensor_copy", rl_r[:, :], l_r[:, :], inc="dve")
                    em.wait("pe", "dve")
                    em.do("pe", "matmul", psA[0:64, 1536:2048], one_r[0:1, 0:64],
                          rl_r[:, :], start=True, stop=True, inc="pe")
                    em.wait("dve", "pe")
                    em.do("dve", "tensor_copy", rlb[:, :], psA[0:64, 1536:2048])
                    em.wait("dve", "st")
                    em.do("dve", "tensor_tensor", ao_t[qt % 2][:, :],
                          psA[0:65, 1024:1536][0:64, 0:512], rlb[:, :], ALU.mult,
                          inc="dve")
                    em.wait("gp", "dve")
                    em.dma("gp", ao_d[hl * 64:(hl + 1) * 64, qt * 512:(qt + 1) * 512],
                           ao_t[qt % 2][:, :], "st")
            em.wait("gp", "st")

        run_sections(nc, ["ld", "st", "pe", "dve", "act"], prog)
    return nc



def build_proj_res(with_bias=False):
    nc = bass.Bass("TRN2", target_bir_lowering=False)
    x_d = nc.dram_tensor("x", [D, TSH], F32R, kind="ExternalInput").ap()
    at_d = nc.dram_tensor("at", [D, TSH], F32R, kind="ExternalInput").ap()
    w_d = nc.dram_tensor("w", [D, D], F32R, kind="ExternalInput").ap()
    if with_bias:
        b_d = nc.dram_tensor("b", [128, NCH], F32, kind="ExternalInput").ap()
    xo_d = nc.dram_tensor("xo", [D, TSH], F32R, kind="ExternalOutput").ap()

    ctx = ExitStack()
    with ctx:
        w_s = ctx.enter_context(nc.sbuf_tensor("w_s", [128, NCH * D], F32R))
        at_s = ctx.enter_context(nc.sbuf_tensor("at_s", [128, NCH * TT], F32R))
        xt = ctx.enter_context(nc.sbuf_tensor("xt", [128, NCH * TT], F32R))
        if with_bias:
            b_s = ctx.enter_context(nc.sbuf_tensor("b_s", [128, NCH], F32))
        psB = ctx.enter_context(nc.psum_tensor("psB", [128, 1024], F32))

        NTT = TSH // TT

        def prog(em: Emitter):
            em.dma("sync", w_s[:, :].rearrange("p (c o) -> p c o", c=NCH),
                   chunked(w_d), "ld")
            if with_bias:
                em.dma("sync", b_s[:, :], b_d, "ld")
            xv, av, xov = chunked(x_d), chunked(at_d), chunked(xo_d)
            for i in range(NTT):
                t0 = i * TT
                if i > 0:
                    em.wait("sync", "st")
                em.dma("sync", xt[:, :].rearrange("p (c t) -> p c t", c=NCH),
                       xv[:, :, t0:t0 + TT], "ld")
                em.dma("sync", at_s[:, :].rearrange("p (c t) -> p c t", c=NCH),
                       av[:, :, t0:t0 + TT], "ld")
                em.wait("pe", "ld")
                for oc in range(NCH):
                    sl = slice((oc % 2) * 512, (oc % 2) * 512 + 512)
                    if oc >= 2:
                        em.wait("pe", "dve")
                    for c in range(NCH):
                        em.do("pe", "matmul", psB[0:128, sl],
                              w_s[:, c * D + oc * 128:c * D + (oc + 1) * 128],
                              at_s[:, c * TT:(c + 1) * TT],
                              start=(c == 0), stop=(c == NCH - 1),
                              inc=("pe" if c == NCH - 1 else None))
                    em.wait("dve", "pe")
                    xsl = slice(oc * TT, (oc + 1) * TT)
                    if with_bias:
                        em.do("dve", "scalar_tensor_tensor", xt[:, xsl],
                              psB[0:128, sl], b_s[:, oc:oc + 1], xt[:, xsl],
                              ALU.add, ALU.add, inc="dve")
                    else:
                        em.do("dve", "tensor_tensor", xt[:, xsl], xt[:, xsl],
                              psB[0:128, sl], ALU.add, inc="dve")
                em.wait("gp", "dve")
                em.dma("gp", xov[:, :, t0:t0 + TT],
                       xt[:, :].rearrange("p (c t) -> p c t", c=NCH), "st")
            em.wait("gp", "st")

        run_sections(nc, ["ld", "st", "pe", "dve", "act"], prog)
    return nc



def build_f0():
    nc = bass.Bass("TRN2", target_bir_lowering=False)
    x_d = nc.dram_tensor("x", [D, TSH], F32R, kind="ExternalInput").ap()
    at_d = nc.dram_tensor("at", [D, 512], F32R, kind="ExternalInput").ap()
    bst_d = nc.dram_tensor("bst", [512, D], F32R, kind="ExternalInput").ap()
    wrt_d = nc.dram_tensor("wrt", [D, 8], F32R, kind="ExternalInput").ap()
    br_d = nc.dram_tensor("br", [8, 1], F32, kind="ExternalInput").ap()
    onec_d = nc.dram_tensor("onec", [128, 1], F32R, kind="ExternalInput").ap()
    oner_d = nc.dram_tensor("oner", [1, 128], F32R, kind="ExternalInput").ap()
    e8_d = nc.dram_tensor("e8", [8, 512], F32R, kind="ExternalInput").ap()
    idn_d = nc.dram_tensor("idn", [128, 128], F32R, kind="ExternalInput").ap()
    xo_d = nc.dram_tensor("xo", [D, TSH], F32R, kind="ExternalOutput").ap()

    ctx = ExitStack()
    with ctx:
        a_t = ctx.enter_context(nc.sbuf_tensor("a_t", [128, NCH * 512], F32R))
        b_st = ctx.enter_context(nc.sbuf_tensor("b_st", [128, 4 * D], F32R))
        wr_t = ctx.enter_context(nc.sbuf_tensor("wr_t", [128, NCH * 8], F32R))
        br_c = ctx.enter_context(nc.sbuf_tensor("br_c", [8, 1], F32))
        one_c = ctx.enter_context(nc.sbuf_tensor("one_c", [128, 1], F32R))
        one_r = ctx.enter_context(nc.sbuf_tensor("one_r", [1, 128], F32R))
        e8_t = ctx.enter_context(nc.sbuf_tensor("e8_t", [8, 512], F32R))
        idn_t = ctx.enter_context(nc.sbuf_tensor("idn_t", [128, 128], F32R))
        xt = ctx.enter_context(nc.sbuf_tensor("xt", [128, NCH * TT], F32R))
        sq = ctx.enter_context(nc.sbuf_tensor("sq", [128, NCH * TT], F32R))
        yt = ctx.enter_context(nc.sbuf_tensor("yt", [128, NCH * TT], F32R))
        hm = ctx.enter_context(nc.sbuf_tensor("hm", [128, 4 * TT], F32R))
        mwbc = ctx.enter_context(nc.sbuf_tensor("mwbc", [128, 4 * TT], F32))
        m_r = ctx.enter_context(nc.sbuf_tensor("m_r", [1, TT], F32))
        msq_r = ctx.enter_context(nc.sbuf_tensor("msq_r", [1, TT], F32))
        t_r = ctx.enter_context(nc.sbuf_tensor("t_r", [1, TT], F32))
        u_r = ctx.enter_context(nc.sbuf_tensor("u_r", [1, TT], F32))
        rstd_r = ctx.enter_context(nc.sbuf_tensor("rstd_r", [1, TT], F32R))
        rm_r = ctx.enter_context(nc.sbuf_tensor("rm_r", [1, TT], F32R))
        lg_s = ctx.enter_context(nc.sbuf_tensor("lg_s", [8, TT], F32R))
        ex_s = ctx.enter_context(nc.sbuf_tensor("ex_s", [128, 32], F32))
        mx_s = ctx.enter_context(nc.sbuf_tensor("mx_s", [128, 4], F32))
        nmx_s = ctx.enter_context(nc.sbuf_tensor("nmx_s", [128, 4], F32))
        sm_s = ctx.enter_context(nc.sbuf_tensor("sm_s", [128, 4], F32))
        rw_s = ctx.enter_context(nc.sbuf_tensor("rw_s", [128, 4], F32))
        msk_s = ctx.enter_context(nc.sbuf_tensor("msk_s", [128, 32], F32))
        mw_s = ctx.enter_context(nc.sbuf_tensor("mw_s", [128, 32], F32R))
        mwt_s = ctx.enter_context(nc.sbuf_tensor("mwt_s", [8, TT], F32R))
        psA = ctx.enter_context(nc.psum_tensor("psA", [128, 2048], F32))
        psB = ctx.enter_context(nc.psum_tensor("psB", [128, 512], F32))
        psT = ctx.enter_context(nc.psum_tensor("psT", [128, 1024], F32R))

        NTT = TSH // TT

        def prog(em: Emitter):
            em.dma("sync", a_t[:, :].rearrange("p (c e) -> p c e", c=NCH), chunked(at_d), "ld")
            em.dma("sync", b_st[:, :].rearrange("p (c o) -> p c o", c=4), chunked(bst_d), "ld")
            em.dma("sync", wr_t[:, :].rearrange("p (c e) -> p c e", c=NCH), chunked(wrt_d), "ld")
            em.dma("sync", br_c[:, :], br_d, "ld")
            em.dma("sync", one_c[:, :], onec_d, "ld")
            em.dma("sync", one_r[:, :], oner_d, "ld")
            em.dma("sync", e8_t[:, :], e8_d, "ld")
            em.dma("sync", idn_t[:, :], idn_d, "ld")
            wld = em.cnt["ld"]

            xv = chunked(x_d)
            xov = chunked(xo_d)

            for i in range(NTT):
                t0 = i * TT
                if i > 0:
                    em.wait("sync", "st")
                em.dma("sync", xt[:, :].rearrange("p (c t) -> p c t", c=NCH),
                       xv[:, :, t0:t0 + TT], "ld")

                em.wait("act", "ld")
                em.do("act", "activation", sq[:, :], xt[:, :], AF.Square, inc="act")
                em.wait("pe", "act")
                for c in range(NCH):
                    em.do("pe", "matmul", psA[0:1, 0:512], one_c[:, :],
                          xt[:, c * TT:(c + 1) * TT], start=(c == 0), stop=(c == NCH - 1))
                for c in range(NCH):
                    em.do("pe", "matmul", psA[0:1, 512:1024], one_c[:, :],
                          sq[:, c * TT:(c + 1) * TT], start=(c == 0), stop=(c == NCH - 1),
                          inc=("pe" if c == NCH - 1 else None))
                em.wait("dve", "pe")
                em.do("dve", "tensor_scalar", m_r[:, :], psA[0:1, 0:512], 1.0 / D, None, ALU.mult)
                em.do("dve", "tensor_tensor", msq_r[:, :], m_r[:, :], m_r[:, :], ALU.mult)
                em.do("dve", "scalar_tensor_tensor", t_r[:, :], psA[0:1, 512:1024],
                      1.0 / D, msq_r[:, :], ALU.mult, ALU.subtract)
                em.do("dve", "tensor_scalar_add", t_r[:, :], t_r[:, :], LN_EPS, inc="dve")
                em.wait("act", "dve")
                em.do("act", "activation", u_r[:, :], t_r[:, :], AF.Ln)
                em.do("act", "activation", rstd_r[:, :], u_r[:, :], AF.Exp, scale=-0.5,
                      inc="act")
                em.wait("dve", "act")
                em.do("dve", "tensor_tensor", rm_r[:, :], m_r[:, :], rstd_r[:, :], ALU.mult,
                      inc="dve")
                em.wait("pe", "dve")
                em.do("pe", "matmul", psA[0:128, 1024:1536], one_r[:, :], rstd_r[:, :],
                      start=True, stop=True)
                em.do("pe", "matmul", psA[0:128, 1536:2048], one_r[:, :], rm_r[:, :],
                      start=True, stop=True, inc="pe")
                em.wait("dve", "pe")
                for c in range(NCH):
                    sl = slice(c * TT, (c + 1) * TT)
                    em.do("dve", "tensor_tensor", yt[:, sl], xt[:, sl],
                          psA[0:128, 1024:1536], ALU.mult)
                    em.do("dve", "tensor_tensor", yt[:, sl], yt[:, sl],
                          psA[0:128, 1536:2048], ALU.subtract,
                          inc=("dve" if c == NCH - 1 else None))

                em.wait("pe", "dve")
                for c in range(NCH):
                    em.do("pe", "matmul", psB[0:8, 0:512], wr_t[:, c * 8:(c + 1) * 8],
                          yt[:, c * TT:(c + 1) * TT], start=(c == 0), stop=(c == NCH - 1),
                          inc=("pe" if c == NCH - 1 else None))
                em.wait("dve", "pe")
                em.do("dve", "tensor_scalar", lg_s[:, :], psB[0:8, 0:512],
                      br_c[0:8, 0:1], None, ALU.add, inc="dve")
                em.wait("pe", "dve")
                for g in range(4):
                    em.do("pe", "transpose", psT[0:128, g * 8:(g + 1) * 8],
                          lg_s[:, g * 128:(g + 1) * 128], idn_t[0:8, 0:8],
                          inc=("pe" if g == 3 else None))
                em.wait("dve", "pe")
                for g in range(4):
                    em.do("dve", "reduce_max", mx_s[:, g:g + 1], psT[0:128, g * 8:(g + 1) * 8],
                          axis=AX)
                em.do("dve", "tensor_scalar", nmx_s[:, :], mx_s[:, :], -1.0, None, ALU.mult,
                      inc="dve")
                em.wait("act", "dve")
                for g in range(4):
                    em.do("act", "activation", ex_s[:, g * 8:(g + 1) * 8],
                          psT[0:128, g * 8:(g + 1) * 8], AF.Exp, bias=nmx_s[:, g:g + 1],
                          accum_out=sm_s[:, g:g + 1], inc=("act" if g == 3 else None))
                em.wait("dve", "act")
                em.do("dve", "reciprocal", rw_s[:, :], sm_s[:, :])
                for g in range(4):
                    em.do("dve", "tensor_scalar", msk_s[:, g * 8:(g + 1) * 8],
                          psT[0:128, g * 8:(g + 1) * 8], mx_s[:, g:g + 1], None, ALU.is_equal)
                    em.do("dve", "tensor_scalar", mw_s[:, g * 8:(g + 1) * 8],
                          msk_s[:, g * 8:(g + 1) * 8], rw_s[:, g:g + 1], 1.0 / 128.0,
                          ALU.mult, ALU.mult, inc=("dve" if g == 3 else None))
                em.wait("pe", "dve")
                for g in range(4):
                    em.do("pe", "transpose", psT[0:8, 512 + g * 128:512 + (g + 1) * 128],
                          mw_s[:, g * 8:(g + 1) * 8], idn_t[:, :],
                          inc=("pe" if g == 3 else None))
                em.wait("dve", "pe")
                em.do("dve", "tensor_copy", mwt_s[:, :], psT[0:8, 512:1024], inc="dve")
                em.wait("pe", "dve")
                for c in range(4):
                    em.do("pe", "matmul", psA[0:128, c * 512:(c + 1) * 512],
                          e8_t[:, c * 128:(c + 1) * 128], mwt_s[:, :],
                          start=True, stop=True, inc=("pe" if c == 3 else None))
                em.wait("dve", "pe")
                em.do("dve", "tensor_copy", mwbc[:, :], psA[0:128, 0:2048], inc="dve")
                em.wait("pe", "dve")
                for ec in range(4):
                    for c in range(NCH):
                        em.do("pe", "matmul", psA[0:128, ec * 512:(ec + 1) * 512],
                              a_t[:, c * 512 + ec * 128:c * 512 + (ec + 1) * 128],
                              yt[:, c * TT:(c + 1) * TT],
                              start=(c == 0), stop=(c == NCH - 1),
                              inc=("pe" if (ec == 3 and c == NCH - 1) else None))
                em.wait("dve", "pe")
                for ec in range(4):
                    sl = slice(ec * 512, (ec + 1) * 512)
                    em.do("dve", "tensor_tensor", hm[:, sl], psA[0:128, sl], mwbc[:, sl],
                          ALU.mult, inc=("dve" if ec == 3 else None))
                for og in range(4):
                    em.wait("pe", "dve")
                    for j in range(4):
                        oc = og * 4 + j
                        for ec in range(4):
                            em.do("pe", "matmul", psA[0:128, j * 512:(j + 1) * 512],
                                  b_st[:, ec * D + oc * 128:ec * D + (oc + 1) * 128],
                                  hm[:, ec * 512:(ec + 1) * 512],
                                  start=(ec == 0), stop=(ec == 3),
                                  inc=("pe" if (j == 3 and ec == 3) else None))
                    em.wait("dve", "pe")
                    for j in range(4):
                        oc = og * 4 + j
                        sl = slice(oc * TT, (oc + 1) * TT)
                        em.do("dve", "tensor_tensor", xt[:, sl], xt[:, sl],
                              psA[0:128, j * 512:(j + 1) * 512], ALU.add,
                              inc=("dve" if j == 3 else None))
                em.wait("gp", "dve")
                em.dma("gp", xov[:, :, t0:t0 + TT],
                       xt[:, :].rearrange("p (c t) -> p c t", c=NCH), "st")
            em.wait("gp", "st")

        run_sections(nc, ["ld", "st", "pe", "dve", "act"], prog)
    return nc



def build_f1(debug_hdn=False, hdn_as_input=False):
    NI = 5632
    NIC = NI // 128
    nc = bass.Bass("TRN2", target_bir_lowering=False)
    x_d = nc.dram_tensor("x", [D, TSH], F32R, kind="ExternalInput").ap()
    gblk_d = nc.dram_tensor("gblk", [NIC, NCH, 128, 128], F32, kind="ExternalInput").ap()
    ublk_d = nc.dram_tensor("ublk", [NIC, NCH, 128, 128], F32, kind="ExternalInput").ap()
    dwt_d = nc.dram_tensor("dwt", [NI, D], F32, kind="ExternalInput").ap()
    gs_d = nc.dram_tensor("gs", [128, NIC], F32, kind="ExternalInput").ap()
    us_d = nc.dram_tensor("us", [128, NIC], F32, kind="ExternalInput").ap()
    ds_d = nc.dram_tensor("ds", [128, NCH], F32, kind="ExternalInput").ap()
    onec_d = nc.dram_tensor("onec", [128, 1], F32R, kind="ExternalInput").ap()
    oner_d = nc.dram_tensor("oner", [1, 128], F32R, kind="ExternalInput").ap()
    onecb_d = nc.dram_tensor("onecb", [128, 1], BF16, kind="ExternalInput").ap()
    xo_d = nc.dram_tensor("xo", [D, TSH], F32R, kind="ExternalOutput").ap()
    hdn_dram = nc.dram_tensor("hdn_scr", [NIC, 128, TSH], BF16,
                              kind=("ExternalInput" if hdn_as_input else
                                    ("ExternalOutput" if debug_hdn else "Internal"))).ap()
    debug_full = False
    if debug_full:
        ydbg_d = nc.dram_tensor("ydbg", [128, NCH * TSH], BF16, kind="ExternalOutput").ap()
        gdbg_d = nc.dram_tensor("gdbg", [128, TSH], BF16, kind="ExternalOutput").ap()
        udbg_d = nc.dram_tensor("udbg", [128, TSH], F32, kind="ExternalOutput").ap()

    ctx = ExitStack()
    with ctx:
        ybig = ctx.enter_context(nc.sbuf_tensor("ybig", [128, NCH * TSH], BF16))
        xt = ctx.enter_context(nc.sbuf_tensor("xt", [128, NCH * TT], F32R))
        sq = ctx.enter_context(nc.sbuf_tensor("sq", [128, NCH * TT], BF16))
        onecb = ctx.enter_context(nc.sbuf_tensor("onecb_s", [128, 1], BF16))
        one_c = ctx.enter_context(nc.sbuf_tensor("one_c", [128, 1], F32R))
        one_r = ctx.enter_context(nc.sbuf_tensor("one_r", [1, 128], F32R))
        gt_bf = [ctx.enter_context(nc.sbuf_tensor(f"gt_bf{i}", [128, NCH * 128], BF16))
                 for i in range(2)]
        ut_bf = [ctx.enter_context(nc.sbuf_tensor(f"ut_bf{i}", [128, NCH * 128], BF16))
                 for i in range(2)]
        gs_s = ctx.enter_context(nc.sbuf_tensor("gs_s", [128, NIC], F32))
        us_s = ctx.enter_context(nc.sbuf_tensor("us_s", [128, NIC], F32))
        ds_s = ctx.enter_context(nc.sbuf_tensor("ds_s", [128, NCH], F32))
        gact = ctx.enter_context(nc.sbuf_tensor("gact", [128, TSH], BF16))
        hdn_sb = [ctx.enter_context(nc.sbuf_tensor(f"hdn_sb{i}", [128, TSH], BF16))
                  for i in range(2)]
        dw_bf = [ctx.enter_context(nc.sbuf_tensor(f"dw_bf{i}", [128, 512], BF16))
                 for i in range(2)]
        hd_in = [ctx.enter_context(nc.sbuf_tensor(f"hd_in{i}", [128, 1024], BF16))
                 for i in range(2)]
        xt2 = ctx.enter_context(nc.sbuf_tensor("xt2", [128, 4 * 1024], F32R))
        smalls = ln_smalls(nc, ctx, "ln")
        psA = ctx.enter_context(nc.psum_tensor("psA", [128, 2048], F32))
        psU = ctx.enter_context(nc.psum_tensor("psU", [128, 2048], F32))

        def prog(em: Emitter):
            em.dma("sync", one_c[:, :], onec_d, "ld")
            em.dma("sync", one_r[:, :], oner_d, "ld")
            em.dma("sync", onecb[:, :], onecb_d, "ld")
            em.dma("sync", gs_s[:, :], gs_d, "ld")
            em.dma("sync", us_s[:, :], us_d, "ld")
            em.dma("sync", ds_s[:, :], ds_d, "ld")
            em.wait("dve", "ld")
            em.do("dve", "tensor_scalar_mul", ds_s[:, :], ds_s[:, :], 0.5, inc="dve")

            xv = chunked(x_d)
            for i in range(TSH // TT):
                t0 = i * TT
                if i > 0:
                    em.wait("sync", "pe")
                    em.wait("sync", "dve")
                em.dma("sync", xt[:, :].rearrange("p (c t) -> p c t", c=NCH),
                       xv[:, :, t0:t0 + TT], "ld")
                em.wait("act", "ld")
                emit_ln(em, xt, sq, None, psA, one_c, one_r, smalls, one_sq=onecb,
                        ysl=lambda c: ybig[:, c * TSH + t0:c * TSH + t0 + TT])

            if debug_full:
                em.wait("gp", "dve")
                em.dma("gp", ydbg_d, ybig[:, :], "st")
            gv = gblk_d.rearrange("i c p f -> i p c f")
            uv = ublk_d.rearrange("i c p f -> i p c f")
            NIC_ = NIC
            for ic in range(NIC_):
                bb = 0
                if ic >= 1:
                    em.wait("gp", "pe")
                em.dma("gp", gt_bf[bb][:, :].rearrange("p (c f) -> p c f", c=NCH),
                       gv[ic], "lg")
                em.dma("gp", ut_bf[bb][:, :].rearrange("p (c f) -> p c f", c=NCH),
                       uv[ic], "lg")
                em.wait("pe", "lg")
                if ic == 0:
                    em.wait("pe", "dve")
                if ic >= 1:
                    em.wait("pe", "act")
                    em.wait("pe", "dve")
                for tc in range(4):
                    for c in range(NCH):
                        em.do("pe", "matmul", psA[0:128, tc * 512:(tc + 1) * 512],
                              gt_bf[bb][:, c * 128:(c + 1) * 128],
                              ybig[:, c * TSH + tc * 512:c * TSH + (tc + 1) * 512],
                              start=(c == 0), stop=(c == NCH - 1))
                for tc in range(4):
                    for c in range(NCH):
                        em.do("pe", "matmul", psU[0:128, tc * 512:(tc + 1) * 512],
                              ut_bf[bb][:, c * 128:(c + 1) * 128],
                              ybig[:, c * TSH + tc * 512:c * TSH + (tc + 1) * 512],
                              start=(c == 0), stop=(c == NCH - 1),
                              inc=("pe" if tc == 3 and c == NCH - 1 else None))
                em.wait("act", "pe")
                em.do("act", "activation", gact[:, :], psA[0:128, :], AF.Silu,
                      scale=gs_s[:, ic:ic + 1], inc="act")
                em.wait("dve", "act")
                em.wait("dve", "st")
                em.do("dve", "scalar_tensor_tensor", hdn_sb[bb][:, :], psU[0:128, :],
                      us_s[:, ic:ic + 1], gact[:, :], ALU.mult, ALU.mult, inc="dve")
                em.wait("gp", "dve")
                if not hdn_as_input:
                    em.dma("gp", hdn_dram[ic], hdn_sb[bb][:, :], "st")
                if debug_full and ic == 0:
                    em.do("dve", "tensor_copy", xt2[:, 0:2048], psU[0:128, :],
                          inc="dve")
                    em.wait("gp", "dve")
                    em.dma("gp", gdbg_d, gact[:, :], "st")
                    em.dma("gp", udbg_d, xt2[:, 0:2048], "st")
                    em.wait("pe", "dve")

            em.wait("sync", "st")
            xov = chunked(xo_d)
            for og in range(4):
                for th in range(2):
                    for ic in range(NIC_):
                        bb = 0
                        if ic >= 1 or (og, th) != (0, 0):
                            em.wait("gp", "pe")
                            em.wait("sync", "pe")
                        em.dma("gp", dw_bf[bb][:, :],
                               dwt_d[ic * 128:(ic + 1) * 128, og * 512:(og + 1) * 512],
                               "lg")
                        em.dma("sync", hd_in[bb][:, :],
                               hdn_dram[ic][:, th * 1024:(th + 1) * 1024], "ld")
                        em.wait("pe", "ld")
                        em.wait("pe", "lg")
                        if ic == 0:
                            em.wait("pe", "dve")
                        for j in range(4):
                            ps = psA if j < 2 else psU
                            col0 = (j % 2) * 1024
                            for tc in range(2):
                                em.do("pe", "matmul",
                                      ps[0:128, col0 + tc * 512:col0 + (tc + 1) * 512],
                                      dw_bf[bb][:, j * 128:(j + 1) * 128],
                                      hd_in[bb][:, tc * 512:(tc + 1) * 512],
                                      start=(ic == 0), stop=(ic == NIC_ - 1),
                                      inc=("pe" if (j == 3 and tc == 1) else None))
                    em.wait("sync", "dve")
                    em.wait("sync", "st")
                    em.dma("sync",
                           xt2[:, :].rearrange("p (c t) -> p c t", c=4),
                           xv[:, og * 4:og * 4 + 4, th * 1024:(th + 1) * 1024], "ld")
                    em.wait("dve", "pe")
                    em.wait("dve", "ld")
                    for j in range(4):
                        oc = og * 4 + j
                        ps = psA if j < 2 else psU
                        col0 = (j % 2) * 1024
                        em.do("dve", "scalar_tensor_tensor",
                              xt2[:, j * 1024:(j + 1) * 1024],
                              ps[0:128, col0:col0 + 1024], ds_s[:, oc:oc + 1],
                              xt2[:, j * 1024:(j + 1) * 1024], ALU.mult, ALU.add,
                              inc=("dve" if j == 3 else None))
                    em.wait("gp", "dve")
                    em.dma("gp", xov[:, og * 4:og * 4 + 4, th * 1024:(th + 1) * 1024],
                           xt2[:, :].rearrange("p (c t) -> p c t", c=4), "st")
            em.wait("gp", "st")

        run_sections(nc, ["ld", "lg", "st", "pe", "dve", "act"], prog)
    return nc



def build_a1a():
    DH = 128
    NH_L = 8
    SC = 1.0 / np.sqrt(128.0)
    nc = bass.Bass("TRN2", target_bir_lowering=False)
    x_d = nc.dram_tensor("x", [D, S], F32R, kind="ExternalInput").ap()
    wqt_d = nc.dram_tensor("wqt", [D, NH_L * DH], F32, kind="ExternalInput").ap()
    wqb_d = nc.dram_tensor("wqb", [128, NH_L], F32, kind="ExternalInput").ap()
    wkvat_d = nc.dram_tensor("wkvat", [D, 512], F32, kind="ExternalInput").ap()
    wkvbt_d = nc.dram_tensor("wkvbt", [512, 2 * NH_L * DH], F32, kind="ExternalInput").ap()
    normw_d = nc.dram_tensor("normw", [128, 4], F32, kind="ExternalInput").ap()
    fc_d = nc.dram_tensor("fc", [128, NH_L], F32, kind="ExternalInput").ap()
    fs_d = nc.dram_tensor("fs", [128, NH_L], F32, kind="ExternalInput").ap()
    onec_d = nc.dram_tensor("onec", [128, 1], F32R, kind="ExternalInput").ap()
    oner_d = nc.dram_tensor("oner", [1, 128], F32R, kind="ExternalInput").ap()
    onecb_d = nc.dram_tensor("onecb", [128, 1], BF16, kind="ExternalInput").ap()
    idnb_d = nc.dram_tensor("idnb", [128, 128], BF16, kind="ExternalInput").ap()
    zmask_d = nc.dram_tensor("zmask", [128, 1024], BF16, kind="ExternalInput").ap()
    ao_d = nc.dram_tensor("ao", [NH_L * DH, S], F32, kind="ExternalOutput").ap()
    q_dram = nc.dram_tensor("q_scr", [NH_L, 128, S], BF16).ap()
    k_dram = nc.dram_tensor("k_scr", [NH_L, 128, S], BF16).ap()
    v_dram = nc.dram_tensor("v_scr", [NH_L, S // 128, 128, 128], BF16).ap()

    ctx = ExitStack()
    with ctx:
        wq_bf = ctx.enter_context(nc.sbuf_tensor("wq_bf", [128, NCH * 1024], BF16))
        wkva_bf = ctx.enter_context(nc.sbuf_tensor("wkva_bf", [128, NCH * 512], BF16))
        wkvb_bf = ctx.enter_context(nc.sbuf_tensor("wkvb_bf", [128, 4 * 2048], BF16))
        wqb_s = ctx.enter_context(nc.sbuf_tensor("wqb_s", [128, NH_L], F32))
        normw_s = ctx.enter_context(nc.sbuf_tensor("normw_s", [128, 4], F32))
        fc_s = ctx.enter_context(nc.sbuf_tensor("fc_s", [128, NH_L], F32))
        fs_s = ctx.enter_context(nc.sbuf_tensor("fs_s", [128, NH_L], F32))
        one_c = ctx.enter_context(nc.sbuf_tensor("one_c", [128, 1], F32R))
        one_r = ctx.enter_context(nc.sbuf_tensor("one_r", [1, 128], F32R))
        onecb = ctx.enter_context(nc.sbuf_tensor("onecb_s", [128, 1], BF16))
        idnb = ctx.enter_context(nc.sbuf_tensor("idnb_s", [128, 128], BF16))
        zmask = ctx.enter_context(nc.sbuf_tensor("zmask_s", [128, 1024], BF16))
        xt = ctx.enter_context(nc.sbuf_tensor("xt", [128, NCH * TT], F32R))
        sq = ctx.enter_context(nc.sbuf_tensor("sq", [128, NCH * TT], BF16))
        yt = ctx.enter_context(nc.sbuf_tensor("yt", [128, NCH * TT], BF16))
        sql = ctx.enter_context(nc.sbuf_tensor("sql", [128, 4 * TT], BF16))
        rrb_s = ctx.enter_context(nc.sbuf_tensor("rrb_s", [128, TT], F32))
        latn = ctx.enter_context(nc.sbuf_tensor("latn", [128, 4 * TT], BF16))
        kvbuf = [ctx.enter_context(nc.sbuf_tensor(f"kvbuf{i}", [128, 512], BF16))
                 for i in range(2)]
        rsc = ctx.enter_context(nc.sbuf_tensor("rsc", [128, 512], BF16))
        rsc2 = ctx.enter_context(nc.sbuf_tensor("rsc2", [128, 512], BF16))
        vtt = ctx.enter_context(nc.sbuf_tensor("vtt", [128, 512], BF16))
        sqq = ctx.enter_context(nc.sbuf_tensor("sqq", [128, 512], BF16))
        qm_row = ctx.enter_context(nc.sbuf_tensor("qm_row", [1, NH_L], F32))
        km_row = ctx.enter_context(nc.sbuf_tensor("km_row", [1, NH_L], F32))
        mtmp = ctx.enter_context(nc.sbuf_tensor("mtmp", [1, 1], F32))
        qs_row = ctx.enter_context(nc.sbuf_tensor("qs_row", [1, NH_L], F32))
        ks_row = ctx.enter_context(nc.sbuf_tensor("ks_row", [1, NH_L], F32))
        negc = ctx.enter_context(nc.sbuf_tensor("negc", [1, NH_L], F32R))
        nC = ctx.enter_context(nc.sbuf_tensor("nC", [128, NH_L], F32))
        ksl = [ctx.enter_context(nc.sbuf_tensor(f"ksl{i}", [128, S], BF16))
               for i in range(1)]
        vsl = [ctx.enter_context(nc.sbuf_tensor(f"vsl{i}", [128, S], BF16))
               for i in range(1)]
        qq = [ctx.enter_context(nc.sbuf_tensor(f"qq{i}", [128, 512], BF16))
              for i in range(2)]
        pt = [ctx.enter_context(nc.sbuf_tensor(f"pt{i}", [128, 512], BF16))
              for i in range(2)]
        l_r = ctx.enter_context(nc.sbuf_tensor("l_r", [1, 512], F32))
        rl_r = ctx.enter_context(nc.sbuf_tensor("rl_r", [1, 512], F32R))
        rlb = ctx.enter_context(nc.sbuf_tensor("rlb", [128, 512], F32))
        ao_t = [ctx.enter_context(nc.sbuf_tensor(f"ao_t{i}", [128, 512], F32))
                for i in range(2)]
        smalls = ln_smalls(nc, ctx, "ln")
        rms = ln_smalls(nc, ctx, "rms")
        psA = ctx.enter_context(nc.psum_tensor("psA", [128, 2048], F32))
        psB = ctx.enter_context(nc.psum_tensor("psB", [128, 1024], F32))
        psN = ctx.enter_context(nc.psum_tensor("psN", [1, 512], F32))
        psTb = ctx.enter_context(nc.psum_tensor("psTb", [128, 512], BF16))

        NTT = S // TT

        def rope(em, buf, hl):
            em.do("dve", "tensor_scalar", rsc[96:128, :], buf[96:128, :],
                  fs_s[96:128, hl:hl + 1], None, ALU.mult)
            em.do("dve", "tensor_scalar", rsc[64:96, :], buf[64:96, :],
                  fs_s[64:96, hl:hl + 1], None, ALU.mult)
            em.do("dve", "tensor_copy", rsc2[64:96, :], rsc[96:128, :])
            em.do("dve", "tensor_copy", rsc2[96:128, :], rsc[64:96, :])
            em.do("dve", "scalar_tensor_tensor", buf[96:128, :], buf[96:128, :],
                  fc_s[96:128, hl:hl + 1], rsc2[96:128, :], ALU.mult, ALU.add)
            em.do("dve", "scalar_tensor_tensor", buf[64:96, :], buf[64:96, :],
                  fc_s[64:96, hl:hl + 1], rsc2[64:96, :], ALU.mult, ALU.subtract,
                  inc="dve")

        def bounds_update(em, buf, row, hl):
            em.do("act", "activation", sqq[:, :], buf[:, :], AF.Square, inc="act")
            em.wait("pe", "act")
            em.do("pe", "matmul", psN[0:1, :], onecb[:, :], sqq[:, :],
                  start=True, stop=True, inc="pe")
            em.wait("dve", "pe")
            em.do("dve", "reduce_max", mtmp[:, :], psN[0:1, :], axis=AX)
            em.do("dve", "tensor_tensor", row[0:1, hl:hl + 1], row[0:1, hl:hl + 1],
                  mtmp[:, :], ALU.max, inc="dve")

        def prog(em: Emitter):
            em.dma("gp", wq_bf[:, :].rearrange("p (c o) -> p c o", c=NCH),
                   chunked(wqt_d), "ld")
            em.dma("gp", wkva_bf[:, :].rearrange("p (c o) -> p c o", c=NCH),
                   chunked(wkvat_d), "ld")
            em.dma("gp", wkvb_bf[:, :].rearrange("p (c o) -> p c o", c=4),
                   chunked(wkvbt_d), "ld")
            em.dma("sync", wqb_s[:, :], wqb_d, "ld")
            em.dma("sync", normw_s[:, :], normw_d, "ld")
            em.dma("sync", fc_s[:, :], fc_d, "ld")
            em.dma("sync", fs_s[:, :], fs_d, "ld")
            em.dma("sync", one_c[:, :], onec_d, "ld")
            em.dma("sync", one_r[:, :], oner_d, "ld")
            em.dma("sync", onecb[:, :], onecb_d, "ld")
            em.dma("sync", idnb[:, :], idnb_d, "ld")
            em.dma("sync", zmask[:, :], zmask_d, "ld")
            em.do("gp", "memset", qm_row[:, :], 0.0)
            em.do("gp", "memset", km_row[:, :], 0.0, inc="dve")

            xv = chunked(x_d)
            for tt in range(NTT):
                t0 = tt * TT
                if tt > 0:
                    em.wait("sync", "pe")
                    em.wait("sync", "dve")
                em.dma("sync", xt[:, :].rearrange("p (c t) -> p c t", c=NCH),
                       xv[:, :, t0:t0 + TT], "ld")
                em.wait("act", "ld")
                emit_ln(em, xt, sq, yt, psA, one_c, one_r, smalls, one_sq=onecb)

                em.wait("pe", "dve")
                for lc in range(4):
                    for c in range(NCH):
                        em.do("pe", "matmul", psA[0:128, lc * 512:(lc + 1) * 512],
                              wkva_bf[:, c * 512 + lc * 128:c * 512 + (lc + 1) * 128],
                              yt[:, c * TT:(c + 1) * TT],
                              start=(c == 0), stop=(c == NCH - 1),
                              inc=("pe" if lc == 3 and c == NCH - 1 else None))
                em.wait("act", "pe")
                em.do("act", "activation", sql[:, :], psA[0:128, :], AF.Square,
                      inc="act")
                em.wait("pe", "act")
                for lc in range(4):
                    em.do("pe", "matmul", psN[0:1, :], onecb[:, :],
                          sql[:, lc * 512:(lc + 1) * 512],
                          start=(lc == 0), stop=(lc == 3),
                          inc=("pe" if lc == 3 else None))
                em.wait("dve", "pe")
                em.do("dve", "tensor_scalar", rms["t"][:, :], psN[0:1, :],
                      1.0 / 512.0, None, ALU.mult)
                em.do("dve", "tensor_scalar_add", rms["t"][:, :], rms["t"][:, :],
                      RMS_EPS, inc="dve")
                em.wait("act", "dve")
                em.do("act", "activation", rms["u"][:, :], rms["t"][:, :], AF.Ln)
                em.do("act", "activation", rms["rstd"][:, :], rms["u"][:, :], AF.Exp,
                      scale=-0.5, inc="act")
                em.wait("pe", "act")
                em.do("pe", "matmul", psB[0:128, 0:512], one_r[:, :],
                      rms["rstd"][:, :], start=True, stop=True, inc="pe")
                em.wait("dve", "pe")
                em.do("dve", "tensor_copy", rrb_s[:, :], psB[0:128, 0:512], inc="dve")
                for lc in range(4):
                    em.do("dve", "scalar_tensor_tensor",
                          latn[:, lc * 512:(lc + 1) * 512],
                          psA[0:128, lc * 512:(lc + 1) * 512], normw_s[:, lc:lc + 1],
                          rrb_s[:, :], ALU.mult, ALU.mult,
                          inc=("dve" if lc == 3 else None))

                for oc in range(16):
                    sl = slice((oc % 2) * 512, (oc % 2) * 512 + 512)
                    em.wait("pe", "dve")
                    for lc in range(4):
                        em.do("pe", "matmul", psB[0:128, sl],
                              wkvb_bf[:, lc * 2048 + oc * 128:lc * 2048 + (oc + 1) * 128],
                              latn[:, lc * 512:(lc + 1) * 512],
                              start=(lc == 0), stop=(lc == 3),
                              inc=("pe" if lc == 3 else None))
                    em.wait("dve", "pe")
                    if oc < 8:
                        hl = oc
                        em.wait("dve", "st")
                        em.do("dve", "tensor_copy", kvbuf[oc % 2][:, :], psB[0:128, sl],
                              inc="dve")
                        rope(em, kvbuf[oc % 2], hl)
                        bounds_update(em, kvbuf[oc % 2], km_row, hl)
                        em.wait("gp", "dve")
                        em.dma("gp", k_dram[hl, :, t0:t0 + TT], kvbuf[oc % 2][:, :],
                               "st")
                    else:
                        hl = oc - 8
                        em.wait("dve", "st")
                        em.do("dve", "tensor_copy", kvbuf[oc % 2][:, :], psB[0:128, sl],
                              inc="dve")
                        em.wait("pe", "dve")
                        for j in range(4):
                            em.do("pe", "transpose", psTb[:, j * 128:(j + 1) * 128],
                                  kvbuf[oc % 2][:, j * 128:(j + 1) * 128], idnb[:, :],
                                  inc=("pe" if j == 3 else None))
                        em.wait("dve", "pe")
                        em.wait("dve", "st")
                        em.do("dve", "tensor_copy", vtt[:, :], psTb[:, 0:512],
                              inc="dve")
                        em.wait("gp", "dve")
                        em.dma("gp",
                               v_dram[hl, tt * 4:(tt + 1) * 4].rearrange("t p f -> p t f"),
                               vtt[:, :].rearrange("p (t f) -> p t f", t=4), "st")

                for oc in range(8):
                    sl = slice((oc % 2) * 512, (oc % 2) * 512 + 512)
                    em.wait("pe", "dve")
                    for c in range(NCH):
                        em.do("pe", "matmul", psB[0:128, sl],
                              wq_bf[:, c * 1024 + oc * 128:c * 1024 + (oc + 1) * 128],
                              yt[:, c * TT:(c + 1) * TT],
                              start=(c == 0), stop=(c == NCH - 1),
                              inc=("pe" if c == NCH - 1 else None))
                    em.wait("dve", "pe")
                    em.wait("dve", "st")
                    em.do("dve", "tensor_scalar", kvbuf[oc % 2][:, :], psB[0:128, sl],
                          wqb_s[:, oc:oc + 1], None, ALU.add, inc="dve")
                    rope(em, kvbuf[oc % 2], oc)
                    bounds_update(em, kvbuf[oc % 2], qm_row, oc)
                    em.wait("gp", "dve")
                    em.dma("gp", q_dram[oc, :, t0:t0 + TT], kvbuf[oc % 2][:, :], "st")

            em.wait("act", "dve")
            em.do("act", "activation", qs_row[:, :], qm_row[:, :], AF.Sqrt)
            em.do("act", "activation", ks_row[:, :], km_row[:, :], AF.Sqrt, inc="act")
            em.wait("dve", "act")
            em.do("dve", "tensor_tensor", negc[:, :], qs_row[:, :], ks_row[:, :],
                  ALU.mult)
            em.do("dve", "tensor_scalar_mul", negc[:, :], negc[:, :], -SC, inc="dve")
            em.wait("pe", "dve")
            em.do("pe", "matmul", psB[0:128, 0:NH_L], one_r[:, :], negc[:, :],
                  start=True, stop=True, inc="pe")
            em.wait("dve", "pe")
            em.do("dve", "tensor_copy", nC[:, :], psB[0:128, 0:NH_L], inc="dve")

            for hl in range(NH_L):
                for qt in range(8):
                    nkc = 4 * (qt + 1)
                    em.wait("sync", "pe")
                    em.wait("sync", "st")
                    em.dma("sync", qq[qt % 2][:, :],
                           q_dram[hl, :, qt * 512:(qt + 1) * 512], "ld")
                    em.dma("sync", ksl[0][:, 0:nkc * 128],
                           k_dram[hl, :, 0:nkc * 128], "ld")
                    em.dma("sync",
                           vsl[0][:, 0:nkc * 128].rearrange("p (t f) -> p t f", f=128),
                           v_dram[hl, 0:nkc].rearrange("t p f -> p t f"), "ld")
                    em.wait("pe", "ld")
                    for kc in range(nkc):
                        koff = kc * 128
                        b = kc % 2
                        ssl = slice(b * 512, b * 512 + 512)
                        if kc >= 2:
                            em.wait("pe", "act")
                        em.do("pe", "matmul", psA[0:128, ssl],
                              ksl[0][:, koff:koff + 128], qq[qt % 2][:, :],
                              start=True, stop=True, inc="pe")
                        em.wait("act", "pe")
                        masked = (koff >= qt * 512)
                        em.do("act", "activation", pt[b][:, :], psA[0:128, ssl], AF.Exp,
                              bias=nC[:, hl:hl + 1], scale=SC, inc="act")
                        if masked:
                            rel = koff - qt * 512
                            em.wait("dve", "act")
                            em.do("dve", "tensor_tensor", pt[b][:, :], pt[b][:, :],
                                  zmask[:, 512 - rel:1024 - rel], ALU.mult, inc="dve")
                            em.wait("pe", "dve")
                        else:
                            em.wait("pe", "act")
                        if kc == 0:
                            em.wait("pe", "dve")
                        em.do("pe", "matmul", psA[0:128, 1024:1536],
                              vsl[0][:, koff:koff + 128], pt[b][:, :],
                              start=(kc == 0), stop=(kc == nkc - 1))
                        em.do("pe", "matmul", psN[0:1, :], onecb[:, :], pt[b][:, :],
                              start=(kc == 0), stop=(kc == nkc - 1), inc="pe")
                    em.wait("dve", "pe")
                    em.do("dve", "tensor_scalar_add", l_r[:, :], psN[0:1, :], 1e-30)
                    em.do("dve", "reciprocal", l_r[:, :], l_r[:, :])
                    em.do("dve", "tensor_copy", rl_r[:, :], l_r[:, :], inc="dve")
                    em.wait("pe", "dve")
                    em.do("pe", "matmul", psA[0:128, 1536:2048], one_r[:, :],
                          rl_r[:, :], start=True, stop=True, inc="pe")
                    em.wait("dve", "pe")
                    em.do("dve", "tensor_copy", rlb[:, :], psA[0:128, 1536:2048])
                    em.wait("dve", "st")
                    em.do("dve", "tensor_tensor", ao_t[qt % 2][:, :],
                          psA[0:128, 1024:1536], rlb[:, :], ALU.mult, inc="dve")
                    em.wait("gp", "dve")
                    em.dma("gp", ao_d[hl * DH:(hl + 1) * DH, qt * 512:(qt + 1) * 512],
                           ao_t[qt % 2][:, :], "st")
            em.wait("gp", "st")

        run_sections(nc, ["ld", "st", "pe", "dve", "act"], prog)
    return nc



def _fm(a):
    return np.ascontiguousarray(np.asarray(a).T)


def _consts():
    return dict(
        onec=np.ones((128, 1), np.float32),
        oner=np.ones((1, 128), np.float32),
        idn=np.eye(128, dtype=np.float32),
    )


def _f0_inputs(A, Bw, Wr, br):
    at = np.ascontiguousarray(np.asarray(A).reshape(512, D).T)
    bst = np.ascontiguousarray(np.transpose(np.asarray(Bw), (0, 2, 1)).reshape(512, D))
    wrt = np.ascontiguousarray(np.asarray(Wr).T)
    e8 = np.zeros((8, 512), np.float32)
    for e in range(8):
        e8[e, e * 64:(e + 1) * 64] = 1.0
    c = _consts()
    return dict(at=at, bst=bst, wrt=wrt, br=np.asarray(br).reshape(8, 1),
                e8=e8, onec=c["onec"], oner=c["oner"], idn=c["idn"])


def _zmask():
    z = np.zeros((128, 1024), BF)
    for k in range(128):
        z[k, k + 512:] = 1.0
    return z


def _a0a_inputs(wq, wk, wv):
    c = _consts()
    selu = np.zeros((128, 1), BF)
    selu[0:64] = 1.0
    sell = np.zeros((128, 1), BF)
    sell[64:128] = 1.0
    base = dict(onec=c["onec"], oner=c["oner"], selu=selu, sell=sell,
                idnb=np.eye(128, dtype=BF), zmask=_zmask(),
                onecb=np.ones((128, 1), BF))
    outs = []
    for hh in range(2):
        outs.append(dict(
            wq=np.ascontiguousarray(np.asarray(wq)[:, hh * 1024:(hh + 1) * 1024]),
            wk=np.ascontiguousarray(np.asarray(wk)[:, hh * 256:(hh + 1) * 256]),
            wv=np.ascontiguousarray(np.asarray(wv)[:, hh * 256:(hh + 1) * 256]),
            **base))
    return outs


def _attn_exchange(nc_a0a, nc_ab, xs, a_in_hh, w_nat, bias_col=None):
    xf = [np.concatenate([xs[2 * b], xs[2 * b + 1]], axis=1) for b in range(B)]
    in_maps = []
    for c in range(NCORE):
        b, hh = c // 2, c % 2
        in_maps.append(dict(x=xf[b], **a_in_hh[hh]))
    res = run_bass_kernel_spmd(nc_a0a, in_maps, core_ids=list(range(NCORE)))
    ao = [res.results[c]["ao"] for c in range(NCORE)]
    in_maps = []
    for c in range(NCORE):
        b, hh = c // 2, c % 2
        at = np.ascontiguousarray(
            np.concatenate([ao[2 * b], ao[2 * b + 1]], axis=0)[:, hh * TSH:(hh + 1) * TSH])
        m = dict(x=xs[c], at=at, w=w_nat)
        if bias_col is not None:
            m["b"] = bias_col
        in_maps.append(m)
    res = run_bass_kernel_spmd(nc_ab, in_maps, core_ids=list(range(NCORE)))
    return [res.results[c]["xo"] for c in range(NCORE)]


def kernel(**inputs):
    x = np.asarray(inputs["L0_attn_x"], np.float32)

    xs = [np.ascontiguousarray(x[c // 2, (c % 2) * TSH:(c % 2 + 1) * TSH, :].T)
          for c in range(NCORE)]

    nc_f0 = build_f0()
    f0_in = _f0_inputs(inputs["L0_ffn_A"], inputs["L0_ffn_B"],
                       inputs["L0_ffn_Wr"], inputs["L0_ffn_br"])

    def run_f0():
        res = run_bass_kernel_spmd(
            nc_f0, [dict(x=xs[c], **f0_in) for c in range(NCORE)],
            core_ids=list(range(NCORE)))
        return [res.results[c]["xo"] for c in range(NCORE)]

    xs = run_f0()
    nc_a0a = build_a0a()
    nc_ab = build_proj_res(with_bias=False)
    a0_in = _a0a_inputs(inputs["L0_attn_wq"], inputs["L0_attn_wk"], inputs["L0_attn_wv"])
    wo = np.ascontiguousarray(np.asarray(inputs["L0_attn_wo"], np.float32))
    xs = _attn_exchange(nc_a0a, nc_ab, xs, a0_in, wo)
    xs = run_f0()

    nc_f1 = build_f1()
    f1_in = _f1_inputs(inputs["L1_ffn_gate_w"], inputs["L1_ffn_up_w"],
                       inputs["L1_ffn_down_w"], inputs["L1_ffn_gate_s"],
                       inputs["L1_ffn_up_s"], inputs["L1_ffn_down_s"])

    def run_f1():
        res = run_bass_kernel_spmd(
            nc_f1, [dict(x=xs[c], **f1_in) for c in range(NCORE)],
            core_ids=list(range(NCORE)))
        return [res.results[c]["xo"] for c in range(NCORE)]

    xs = run_f1()
    nc_a1a = build_a1a()
    nc_a1b = build_proj_res(with_bias=True)
    a1_in = _a1a_inputs(inputs["L1_attn_wq_w"], inputs["L1_attn_wq_b"],
                        inputs["L1_attn_wkv_a_w"], inputs["L1_attn_wkv_b_w"],
                        inputs["L1_attn_norm_w"], inputs["L1_attn_freqs"])
    owt = np.ascontiguousarray(np.asarray(inputs["L1_attn_out_w"], np.float32).T)
    ob = np.ascontiguousarray(np.asarray(inputs["L1_attn_out_b"],
                                         np.float32).reshape(NCH, 128).T)
    xs = _attn_exchange(nc_a1a, nc_a1b, xs, a1_in, owt, bias_col=ob)
    xs = run_f1()

    out = np.empty((B, S, D), np.float32)
    for c in range(NCORE):
        out[c // 2, (c % 2) * TSH:(c % 2 + 1) * TSH, :] = xs[c].T
    return out


_ROPE_PERM = np.concatenate([np.arange(64),
                             64 + 2 * np.arange(32),
                             65 + 2 * np.arange(32)])


def _rope_tile(coef):
    t = np.zeros((128, coef.shape[1]), np.float32)
    t[64:96] = coef
    t[96:128] = coef
    return np.ascontiguousarray(t)


def _f1_inputs(gate_w, up_w, down_w, gate_s, up_s, down_s):
    NI = 5632
    gate_w = np.asarray(gate_w, np.float32)
    up_w = np.asarray(up_w, np.float32)
    down_w = np.asarray(down_w, np.float32)
    gblk = np.ascontiguousarray(
        gate_w.T.reshape(NCH, 128, NI // 128, 128).transpose(2, 0, 1, 3))
    ublk = np.ascontiguousarray(
        up_w.T.reshape(NCH, 128, NI // 128, 128).transpose(2, 0, 1, 3))
    dwt = np.ascontiguousarray(down_w.T)
    c = _consts()
    return dict(gblk=gblk, ublk=ublk, dwt=dwt,
                gs=np.ascontiguousarray(np.asarray(gate_s, np.float32).reshape(NI // 128, 128).T),
                us=np.ascontiguousarray(np.asarray(up_s, np.float32).reshape(NI // 128, 128).T),
                ds=np.ascontiguousarray(np.asarray(down_s, np.float32).reshape(NCH, 128).T),
                onec=c["onec"], oner=c["oner"], onecb=np.ones((128, 1), BF))


def _a1a_inputs(wq_w, wq_b, wkv_a_w, wkv_b_w, norm_w, freqs):
    wq_w = np.asarray(wq_w, np.float32)
    wq_b = np.asarray(wq_b, np.float32)
    wkv_a_w = np.asarray(wkv_a_w, np.float32)
    wkv_b_w = np.asarray(wkv_b_w, np.float32)
    norm_w = np.asarray(norm_w, np.float32)
    freqs = np.asarray(freqs, np.float32)
    c = _consts()
    base = dict(wkvat=np.ascontiguousarray(wkv_a_w.T),
                normw=np.ascontiguousarray(norm_w.reshape(4, 128).T),
                onec=c["onec"], oner=c["oner"], onecb=np.ones((128, 1), BF),
                idnb=np.eye(128, dtype=BF), zmask=_zmask())
    outs = []
    for hh in range(2):
        heads = list(range(hh * 8, (hh + 1) * 8))
        qrows = np.concatenate([wq_w[h * 128:(h + 1) * 128][_ROPE_PERM] for h in heads])
        qb = np.concatenate([wq_b[h * 128:(h + 1) * 128][_ROPE_PERM] for h in heads])
        krows = np.concatenate([wkv_b_w[h * 128:(h + 1) * 128][_ROPE_PERM]
                                for h in heads])
        vrows = np.concatenate([wkv_b_w[2048 + h * 128:2048 + (h + 1) * 128]
                                for h in heads])
        outs.append(dict(
            wqt=np.ascontiguousarray(qrows.T),
            wqb=np.ascontiguousarray(qb.reshape(8, 128).T),
            wkvbt=np.ascontiguousarray(np.concatenate([krows, vrows]).T),
            fc=_rope_tile(freqs[heads, :, 0].T),
            fs=_rope_tile(freqs[heads, :, 1].T),
            **base))
    return outs
